# version 05 (final)
# baseline (speedup 1.0000x reference)
from contextlib import ExitStack

import numpy as np

import concourse.bass as bass
import concourse.mybir as mybir
from concourse.bass_utils import run_bass_kernel_spmd

F32 = mybir.dt.float32
BF16 = mybir.dt.bfloat16
F8 = mybir.dt.float8e4
AL = mybir.AluOpType
AF = mybir.ActivationFunctionType

D = 2048
NH = 16
I = 6144
CT = 2048
T = 512
NMT = CT // T
NTT = T // 128
FC = D // 128
NPT = CT // 128
NDB = 16
EPS = 1e-5
LN2C = 0.6931471805599453
TINY = float(np.finfo(np.float32).tiny)

ENGS = ("sync", "gpsimd", "vector", "scalar", "tensor")


class Sem:
    def __init__(self, handle, name, owner, is_dma):
        self.h = handle
        self.name = name
        self.owner = owner
        self.is_dma = is_dma
        self.n = 0


class Emitter:
    def __init__(self):
        self.ops = {e: [] for e in ENGS}
        self.last_wait = {e: {} for e in ENGS}
        self.lanes = {}
        self.lane_rr = {e: 0 for e in ENGS}

    def on(self, e, fn):
        self.ops[e].append(fn)

    def wait(self, e, tok, for_dma=False):
        if tok is None:
            return
        sem, val = tok
        if sem.owner == e and not sem.is_dma and not for_dma:
            return
        if self.last_wait[e].get(sem.name, -1) >= val:
            return
        self.last_wait[e][sem.name] = val
        self.on(e, lambda eng, s=sem, v=val: eng.wait_ge(s.h, v))

    def waits(self, e, toks, for_dma=False):
        for t in toks:
            if t is None:
                continue
            if isinstance(t, list):
                self.waits(e, t, for_dma=for_dma)
            else:
                self.wait(e, t, for_dma=for_dma)

    def dma(self, e, sem, out_ap, in_ap, waits=(), transpose=False):
        lanes = self.lanes[e]
        li = self.lane_rr[e]
        self.lane_rr[e] = (li + 1) % len(lanes)
        lane = lanes[li]
        self.waits(e, list(waits) + [(lane, lane.n)], for_dma=True)
        lane.n += 16
        if transpose:
            self.on(e, lambda eng, o=out_ap, i=in_ap, s=lane:
                    eng.dma_start(out=o, in_=i, transpose=True).then_inc(s.h, 16))
        else:
            self.on(e, lambda eng, o=out_ap, i=in_ap, s=lane:
                    eng.dma_start(out=o, in_=i).then_inc(s.h, 16))
        return (lane, lane.n)

    def op(self, e, sem, fn, waits=()):
        self.waits(e, waits)
        sem.n += 1
        self.on(e, lambda eng, f=fn, s=sem: f(eng).then_inc(s.h, 1))
        return (sem, sem.n)


def build_program(debug=False):
    nc = bass.Bass("TRN2", target_bir_lowering=False,
                   detect_race_conditions=False)

    din = {}
    for name, shape, dt in [
        ("x_main", [CT, D], F32), ("x_pre", [CT, D], F32),
        ("wq", [D, D], F32), ("wv", [D, D], F32), ("wg", [D, D], F32), ("wo", [D, D], F32),
        ("gate_w", [D, I], F32), ("up_w", [D, I], F32), ("down_w", [I, D], F32),
        ("beta_b", [128, NH], F32),
        ("gam_b", [128, D], F32), ("bet_b", [128, D], F32),
        ("gs_pp", [128, 48], F32), ("us_pp", [128, 48], F32), ("ds_pp", [128, 16], F32),
    ]:
        din[name] = nc.dram_tensor(name, shape, dt, kind="ExternalInput").ap()

    out_ap = nc.dram_tensor("out", [CT, D], F32, kind="ExternalOutput").ap()

    s_wT = {w: nc.dram_tensor(f"s_{w}T", [D, D], BF16).ap()
            for w in ("wq", "wv", "wg", "wo")}
    s_g8 = nc.dram_tensor("s_g8", [24 * 128, 4096], F8).ap()
    s_u8 = nc.dram_tensor("s_u8", [24 * 128, 4096], F8).ap()
    s_d8 = nc.dram_tensor("s_d8", [NDB * 128, 6144], F8).ap()

    dbg = {}
    if debug:
        for name, shape, dt in [
            ("dbg_xnF", [128, FC * T], BF16), ("dbg_v", [128, FC * T], BF16),
            ("dbg_ret", [128, FC * T], BF16), ("dbg_h2", [128, FC * T], BF16),
            ("dbg_state", [128, NH], F32), ("dbg_yt", [128, 256], BF16),
            ("dbg_x2", [128, NTT * D], BF16), ("dbg_fuse", [128, 48 * T], BF16),
            ("dbg_lam", [128, NH], F32), ("dbg_rt", [128, 2 * NH], BF16),
            ("dbg_xnp", [128, NTT * D], BF16),
        ]:
            dbg[name] = nc.dram_tensor(name, shape, dt, kind="ExternalOutput").ap()

    with ExitStack() as es:
        def sbt(name, shape, dt):
            return es.enter_context(nc.sbuf_tensor(name, shape, dt))

        c_beta = sbt("c_beta", [128, NH], F32)
        c_lam = sbt("c_lam", [128, NH], F32)
        c_loglam = sbt("c_loglam", [128, NH], F32)
        c_state = sbt("c_state", [128, NH], F32)
        c_gam = sbt("c_gam", [128, D], BF16)
        c_bet = sbt("c_bet", [128, D], BF16)
        c_gs = sbt("c_gs", [128, 48], F32)
        c_us = sbt("c_us", [128, 48], F32)
        c_ds = sbt("c_ds", [128, 16], F32)
        st_a = sbt("st_a", [128, 64], F32)
        c_yt = sbt("c_yt", [128, 256], BF16)
        c_yt32 = sbt("c_yt32", [128, 256], F32)
        act_dump = sbt("act_dump", [128, D], BF16)
        t_tmp1 = sbt("t_tmp1", [128, D], BF16)
        t_tmp2 = sbt("t_tmp2", [128, D], BF16)
        rt_buf = sbt("rt_buf", [128, 2 * NH], BF16)
        e_buf = sbt("e_buf", [128, NH], F32)
        r_col = sbt("r_col", [128, 2], F32)

        xT = sbt("xT", [128, NTT * D], F32)
        p1 = sbt("p1", [128, NTT * D], BF16)
        p2 = sbt("p2", [128, NTT * D], BF16)
        fA = sbt("fA", [128, FC * T], BF16)
        arena = sbt("arena", [128, 48 * T], BF16)
        vF = arena[:, 0:FC * T]
        retF = arena[:, FC * T:2 * FC * T]
        gF = arena[:, 2 * FC * T:3 * FC * T]
        hF = vF
        h2F = retF
        fuse = arena[:, :]
        wA = sbt("wA", [128, 2 * D], BF16)
        wG = sbt("wG", [128, 2 * 4096], F8)
        wU = sbt("wU", [128, 2 * 4096], F8)
        wD = sbt("wD", [128, 2 * 6144], F8)
        aoc = sbt("aoc", [128, 2 * T], BF16)
        gtmp = sbt("gtmp", [128, 2 * T], BF16)
        wnat = sbt("wnat", [128, 2 * D], BF16)
        wtt = sbt("wtt", [128, 2 * D], BF16)

        pm = [es.enter_context(nc.psum_tensor(f"pm{i}", [128, 512], F32)) for i in range(4)]
        psml = es.enter_context(nc.psum_tensor("psml", [128, 256], F32))
        psml2 = es.enter_context(nc.psum_tensor("psml2", [128, 256], F32))

        sems = {}
        for name, owner, isd in [("s_pe", "tensor", False), ("s_dv", "vector", False),
                                 ("s_ac", "scalar", False), ("s_gc", "gpsimd", False)]:
            sems[name] = Sem(es.enter_context(nc.semaphore(name)), name, owner, isd)
        s_pe, s_dv, s_ac, s_gc = sems["s_pe"], sems["s_dv"], sems["s_ac"], sems["s_gc"]
        s_sy = s_gp = s_tp = None

        E = Emitter()
        NLANES = 8
        for qe in ("sync", "gpsimd", "scalar"):
            E.lanes[qe] = [Sem(es.enter_context(nc.semaphore(f"L_{qe}{i}")),
                               f"L_{qe}{i}", qe, True) for i in range(NLANES)]

        xT_t = [xT[:, tt * D:(tt + 1) * D] for tt in range(NTT)]
        p1_t = [p1[:, tt * D:(tt + 1) * D] for tt in range(NTT)]
        p2_t = [p2[:, tt * D:(tt + 1) * D] for tt in range(NTT)]
        fA_r = fA[:, :].rearrange("p (c t) -> p c t", c=FC)
        p1_rT = p1[:, :].rearrange("p (tt f) -> p tt f", tt=NTT)
        p2_rT = p2[:, :].rearrange("p (tt f) -> p tt f", tt=NTT)

        def mm(out_view, lhsT, rhs, start, stop, waits=()):
            return E.op("tensor", s_pe,
                        lambda eng, o=out_view, l=lhsT, r=rhs, st=start, sp=stop:
                        eng.matmul(o, l, r, start=st, stop=sp),
                        waits=waits)

        def barrier():
            toks = [(s_pe, s_pe.n), (s_dv, s_dv.n), (s_ac, s_ac.n), (s_gc, s_gc.n)]
            for qe in ("sync", "gpsimd", "scalar"):
                toks += [(ln_, ln_.n) for ln_ in E.lanes[qe]]
            for e in ENGS:
                for t in toks:
                    sem, val = t
                    if sem.owner == e and not sem.is_dma:
                        continue
                    E.wait(e, t, for_dma=True)

        def pguard(e):
            if e == "vector":
                E.op("vector", s_dv, lambda eng:
                     eng.tensor_copy(act_dump[:, 1024:2048].bitcast(F32),
                                     act_dump[:, 0:1024].bitcast(F32)))
            else:
                E.op("scalar", s_ac, lambda eng:
                     eng.copy(act_dump[:, 1024:2048], act_dump[:, 0:1024]))

        def pread(e, sem, fn, waits):
            E.waits(e, waits)
            pguard(e)
            return E.op(e, sem, fn)

        def vsettle():
            return E.op("vector", s_dv, lambda eng:
                        eng.tensor_copy(act_dump[:, 1024:2048].bitcast(F32),
                                        act_dump[:, 0:1024].bitcast(F32)))

        slot_free = [None, None, None, None]
        wA_free = [None, None]
        mm_slot = [0]

        def next_slot():
            s = mm_slot[0]
            mm_slot[0] = (s + 1) % 4
            return s

        ln_state = {"base": 0, "free": {0: None, 24: None}}

        def ln_group(tiles, want_negms=False):
            base = ln_state["base"]
            ln_state["base"] = 24 - base
            prev_free = ln_state["free"][base]
            nt = len(tiles)
            sum_c = st_a[:, base:base + nt]
            sq_c = st_a[:, base + nt:base + 2 * nt]
            last_d = last_a = None
            for i, (ap, rtok) in enumerate(tiles):
                last_d = E.op("vector", s_dv,
                              lambda eng, a=ap, sc=sum_c[:, i:i + 1]:
                              eng.tensor_scalar(a, a, 1.0, 0.0, AL.mult, AL.add,
                                                accum_out=sc),
                              waits=[rtok, prev_free])
                last_a = E.op("scalar", s_ac,
                              lambda eng, a=ap, sc=sq_c[:, i:i + 1]:
                              eng.activation(act_dump[:, :], a, AF.Square, accum_out=sc),
                              waits=[rtok, prev_free, last_d])
            b2 = base + 2 * nt
            mu_c = st_a[:, b2:b2 + nt]
            rs_c = st_a[:, b2 + nt:b2 + 2 * nt]
            tmp_c = st_a[:, b2 + 2 * nt:b2 + 3 * nt]
            E.op("vector", s_dv, lambda eng, b=base:
                 eng.tensor_copy(st_a[:, 48:49], st_a[:, b:b + 1]))
            E.op("vector", s_dv, lambda eng, o=mu_c, i_=sum_c:
                 eng.tensor_single_scalar(o, i_, 1.0 / D, AL.mult), waits=[last_d])
            E.op("vector", s_dv, lambda eng, o=tmp_c, i_=sq_c:
                 eng.tensor_single_scalar(o, i_, 1.0 / D, AL.mult), waits=[last_a])
            E.op("vector", s_dv, lambda eng, o=rs_c, a=mu_c:
                 eng.tensor_mul(o, a, a))
            E.op("vector", s_dv, lambda eng, o=tmp_c, a=tmp_c, b=rs_c:
                 eng.tensor_sub(o, a, b))
            tv = E.op("vector", s_dv, lambda eng, o=tmp_c:
                      eng.tensor_single_scalar(o, o, EPS, AL.add))
            ts_ = E.op("scalar", s_ac, lambda eng, o=rs_c, i_=tmp_c:
                       eng.activation(o, i_, AF.Sqrt), waits=[tv])
            ts_ = E.op("scalar", s_ac, lambda eng:
                       eng.copy(act_dump[:, 1024:1536], act_dump[:, 0:512]))
            tok = E.op("vector", s_dv, lambda eng, o=rs_c:
                       eng.reciprocal(o, o), waits=[ts_])
            negms_c = None
            if want_negms:
                negms_c = st_a[:, b2 + 3 * nt:b2 + 4 * nt]
                E.op("vector", s_dv, lambda eng, o=negms_c, a=mu_c, b=rs_c:
                     eng.tensor_mul(o, a, b))
                tok = E.op("vector", s_dv, lambda eng, o=negms_c:
                           eng.tensor_single_scalar(o, o, -1.0, AL.mult))
            for _k in range(8):
                tok = E.op("vector", s_dv, lambda eng, b=base, k=_k:
                           eng.tensor_copy(st_a[:, 48 + (k % 2):49 + (k % 2)],
                                           st_a[:, b:b + 1]))
            ln_state["free"][base] = tok
            return mu_c, rs_c, negms_c, tok

        E.op("vector", s_dv, lambda eng: eng.memzero(st_a[:, 48:56]))
        E.op("scalar", s_ac, lambda eng: eng.memzero(act_dump[:, :]))
        tok_beta = E.dma("sync", s_sy, c_beta[:, :], din["beta_b"])
        tok_g32 = E.dma("sync", s_sy, xT_t[0], din["gam_b"])
        tok_gamc = E.op("vector", s_dv, lambda eng:
                        eng.tensor_copy(c_gam[:, :], xT_t[0]), waits=[tok_g32])
        tok_b32 = E.dma("sync", s_sy, xT_t[1], din["bet_b"])
        tok_betc = E.op("vector", s_dv, lambda eng:
                        eng.tensor_copy(c_bet[:, :], xT_t[1]), waits=[tok_b32])
        tok_gs = E.dma("sync", s_sy, c_gs[:, :], din["gs_pp"])
        tok_us = E.dma("sync", s_sy, c_us[:, :], din["us_pp"])
        tok_ds = E.dma("sync", s_sy, c_ds[:, :], din["ds_pp"])

        tok_l1 = E.op("scalar", s_ac, lambda eng:
                      eng.activation(c_lam[:, :], c_beta[:, :], AF.Exp, scale=-LN2C),
                      waits=[tok_beta])
        E.op("vector", s_dv, lambda eng:
             eng.tensor_scalar(c_lam[:, :], c_lam[:, :], -1.0, 1.0, AL.mult, AL.add),
             waits=[tok_l1])
        t1 = E.op("vector", s_dv, lambda eng:
                  eng.tensor_single_scalar(e_buf[:, :], c_lam[:, :],
                                           -(1.0 - 1e-9), AL.add))
        t2 = E.op("scalar", s_ac, lambda eng:
                  eng.activation(c_loglam[:, :], e_buf[:, :], AF.Relu), waits=[t1])
        t3 = E.op("vector", s_dv, lambda eng:
                  eng.tensor_sub(c_lam[:, :], c_lam[:, :], c_loglam[:, :]),
                  waits=[t2])
        t4 = E.op("vector", s_dv, lambda eng:
                  eng.tensor_scalar(e_buf[:, :], c_lam[:, :], -1.0, TINY,
                                    AL.mult, AL.add))
        t5 = E.op("scalar", s_ac, lambda eng:
                  eng.activation(c_loglam[:, :], e_buf[:, :], AF.Relu), waits=[t4])
        tok_lam = E.op("vector", s_dv, lambda eng:
                       eng.tensor_add(c_lam[:, :], c_lam[:, :], c_loglam[:, :]),
                       waits=[t5])
        tok_ll = E.op("scalar", s_ac, lambda eng:
                      eng.activation(c_loglam[:, :], c_lam[:, :], AF.Ln), waits=[tok_lam])
        tok_io = E.op("gpsimd", s_gc, lambda eng:
                      eng.iota(r_col[:, 0:1], [[1, 1]], channel_multiplier=1,
                               allow_small_or_imprecise_dtypes=True))
        tok_rc = E.op("vector", s_dv, lambda eng:
                      eng.tensor_scalar(r_col[:, 0:1], r_col[:, 0:1], -1.0, float(CT - 1),
                                        AL.mult, AL.add), waits=[tok_io])

        xpre_r = din["x_pre"].rearrange("(n p) d -> n p d", p=128)
        tok_ytz = E.op("vector", s_dv, lambda eng: eng.memzero(c_yt32[:, :]))
        yt_bank_free = [tok_ytz, tok_ytz]
        p1_free = [None] * NTT
        xT_free = [tok_gamc, tok_betc, None, None]
        rt_free = [None, None]
        ebuf_free = [tok_lam]
        for pb in range(NPT // NTT):
            tiles = []
            lds = []
            for j in range(NTT):
                c = pb * NTT + j
                ld = E.dma("gpsimd", s_gp, xT_t[j], xpre_r[c, :, :], waits=[xT_free[j]])
                lds.append(ld)
                tiles.append((xT_t[j], ld))
            mu_c, rs_c, _, _ = ln_group(tiles)
            for j in range(NTT):
                c = pb * NTT + j
                rsl = rt_buf[:, (c % 2) * NH:(c % 2 + 1) * NH]
                E.op("vector", s_dv, lambda eng, cc=c:
                     eng.tensor_scalar(r_col[:, 1:2], r_col[:, 0:1], float(-128.0 * cc),
                                       0.0, AL.add, AL.add),
                     waits=[tok_rc, ebuf_free[0]])
                tok_ap = E.op("vector", s_dv,
                              lambda eng, jj=j, m=mu_c[:, j:j + 1], r=rs_c[:, j:j + 1]:
                              eng.tensor_scalar(p1_t[jj], xT_t[jj], m, r,
                                                AL.subtract, AL.mult),
                              waits=[p1_free[j]])
                xT_free[j] = tok_ap
                tok_eb = E.op("vector", s_dv, lambda eng:
                              eng.tensor_scalar(e_buf[:, :], c_loglam[:, :],
                                                r_col[:, 1:2], 0.0, AL.mult, AL.add),
                              waits=[tok_ll])
                tok_rt = E.op("scalar", s_ac, lambda eng, r=rsl:
                              eng.activation(r, e_buf[:, :], AF.Exp),
                              waits=[tok_eb, rt_free[c % 2]])
                ebuf_free[0] = tok_rt
                pbank = psml if (c % 2 == 0) else psml2
                last = None
                for fb in range(FC):
                    last = mm(pbank[:, fb * 16:(fb + 1) * 16],
                              p1_t[j][:, fb * 128:(fb + 1) * 128], rsl,
                              True, True,
                              waits=[tok_ap, tok_rt, yt_bank_free[c % 2]]
                              if fb == 0 else ())
                rt_free[c % 2] = last
                p1_free[j] = last
                yt_bank_free[c % 2] = pread(
                    "vector", s_dv, lambda eng, pb=pbank:
                    eng.tensor_add(c_yt32[:, :], c_yt32[:, :], pb[:, :]),
                    [last])
        yt_last = yt_bank_free[(NPT - 1) % 2]
        dbg_hold0 = {}
        tok_yt = E.op("vector", s_dv, lambda eng:
                      eng.tensor_copy(c_yt[:, :], c_yt32[:, :]), waits=[yt_last])
        tok_yt = E.op("vector", s_dv, lambda eng:
                      eng.tensor_copy(act_dump[:, 1536:2048].bitcast(F32),
                                      act_dump[:, 0:512].bitcast(F32)))
        if debug:
            E.dma("sync", s_sy, dbg["dbg_yt"], c_yt[:, :], waits=[tok_yt])
            dbg_hold0["rt"] = E.dma("sync", s_sy, dbg["dbg_rt"], rt_buf[:, :],
                                    waits=[yt_last])
            dbg_hold0["p1"] = E.dma("sync", s_sy, dbg["dbg_xnp"], p1[:, :],
                                    waits=[yt_last])

        wnat_free = [None, None]
        wtt_free = [None, None]
        wstore_tok = {}

        def prep_attn(wname):
            toks = []
            src = din[wname].rearrange("(n p) d -> n p d", p=128)
            for oc in range(FC):
                sl = oc % 2
                nat = wnat[:, sl * D:(sl + 1) * D]
                wt = wtt[:, sl * D:(sl + 1) * D]
                t_ld = E.dma("gpsimd", s_gp, nat, src[oc, :, :], waits=[wnat_free[sl]])
                t_tp = E.dma("scalar", s_tp,
                             wt.rearrange("p (c o) -> p c o", c=FC),
                             nat.rearrange("o (c i) -> o c i", c=FC),
                             waits=[t_ld, wtt_free[sl]], transpose=True)
                wnat_free[sl] = t_tp
                t_st = E.dma("sync", s_sy, s_wT[wname][oc * 128:(oc + 1) * 128, :], wt,
                             waits=[t_tp])
                wtt_free[sl] = t_st
                toks.append(t_st)
            wstore_tok[wname] = toks

        f8_store_tok = {"g": [], "u": [], "d": []}

        def prep_ffn():
            gsrc = din["gate_w"].rearrange("(c p) o -> p c o", p=128)
            usrc = din["up_w"].rearrange("(c p) o -> p c o", p=128)
            dsrc = din["down_w"].rearrange("(c p) o -> p c o", p=128)
            wg_free = [None, None]
            wu_free = [None, None]
            wd_free = [None, None]
            for b in range(24):
                sl = b % 2
                gbuf = wG[:, sl * 4096:(sl + 1) * 4096]
                ubuf = wU[:, sl * 4096:(sl + 1) * 4096]
                t1 = E.dma("gpsimd", s_gp,
                           gbuf.rearrange("p (c o) -> p c o", c=FC),
                           gsrc[:, :, b * 256:(b + 1) * 256], waits=[wg_free[sl]])
                t2 = E.dma("gpsimd", s_gp,
                           s_g8[b * 128:(b + 1) * 128, :].bitcast(mybir.dt.uint8),
                           gbuf.bitcast(mybir.dt.uint8), waits=[t1])
                wg_free[sl] = t2
                f8_store_tok["g"].append(t2)
                t3 = E.dma("gpsimd", s_gp,
                           ubuf.rearrange("p (c o) -> p c o", c=FC),
                           usrc[:, :, b * 256:(b + 1) * 256], waits=[wu_free[sl]])
                t4 = E.dma("gpsimd", s_gp,
                           s_u8[b * 128:(b + 1) * 128, :].bitcast(mybir.dt.uint8),
                           ubuf.bitcast(mybir.dt.uint8), waits=[t3])
                wu_free[sl] = t4
                f8_store_tok["u"].append(t4)
            for ob in range(NDB):
                sl = ob % 2
                dbuf = wD[:, sl * 6144:(sl + 1) * 6144]
                t1 = E.dma("gpsimd", s_gp,
                           dbuf.rearrange("p (c o) -> p c o", c=48),
                           dsrc[:, :, ob * 128:(ob + 1) * 128], waits=[wd_free[sl]])
                t2 = E.dma("gpsimd", s_gp,
                           s_d8[ob * 128:(ob + 1) * 128, :].bitcast(mybir.dt.uint8),
                           dbuf.bitcast(mybir.dt.uint8), waits=[t1])
                wd_free[sl] = t2
                f8_store_tok["d"].append(t2)
            for sl in range(2):
                life["wG"][sl] = wg_free[sl]
                life["wU"][sl] = wu_free[sl]
                life["wD"][sl] = wd_free[sl]

        def attn_wblock(wname, oc):
            sl = oc % 2
            buf = wA[:, sl * D:(sl + 1) * D]
            tok = E.dma("sync", s_sy, buf, s_wT[wname][oc * 128:(oc + 1) * 128, :],
                        waits=[wA_free[sl], wstore_tok[wname][oc]])
            return buf, tok, sl

        def proj_group(wbuf, rhs_src, extra_waits=()):
            s = next_slot()
            ov = pm[s][:, 0:512]
            toks = list(extra_waits) + [slot_free[s]]
            last = None
            for ic in range(FC):
                last = mm(ov, wbuf[:, ic * 128:(ic + 1) * 128],
                          rhs_src[:, ic * T:(ic + 1) * T],
                          start=(ic == 0), stop=(ic == FC - 1),
                          waits=toks if ic == 0 else ())
            return ov, last, s

        life = {
            "xT": [None] * NTT, "p1": [None] * NTT, "p1u": None, "p2u": None,
            "fA": None, "vF": None, "retF": None, "gF": None, "fuse": None,
            "aoc": [None, None], "gtmp": [None, None],
            "wG": [None, None], "wU": [None, None], "wD": [None, None],
            "t2": None, "state": None,
        }
        for j in range(NTT):
            life["xT"][j] = xT_free[j]
            life["p1"][j] = p1_free[j]
        life["p1u"] = yt_last

        xmain_r = din["x_main"].rearrange("(n p) d -> n p d", p=128)
        out_r = out_ap.rearrange("(n p) d -> n p d", p=128)
        dbg_hold = {}

        for mt in range(NMT):
            x_ld = []
            for tt in range(NTT):
                ld = E.dma("sync", s_sy, xT_t[tt], xmain_r[mt * NTT + tt, :, :],
                           waits=[life["xT"][tt]])
                x_ld.append(ld)
            mu_c, rs_c, _, _ = ln_group(list(zip(xT_t, x_ld)))
            ap_tok = []
            for tt in range(NTT):
                t = E.op("vector", s_dv,
                         lambda eng, j=tt, m=mu_c[:, tt:tt + 1], r=rs_c[:, tt:tt + 1]:
                         eng.tensor_scalar(p1_t[j], xT_t[j], m, r, AL.subtract, AL.mult),
                         waits=[life["p1"][tt], life["p1u"],
                                dbg_hold0.get("p1") if mt == 0 else None])
                ap_tok.append(t)
            ap_tok.append(vsettle())
            xn_tp = []
            for tt in range(NTT):
                t = E.dma("scalar", s_tp,
                          fA_r[:, :, tt * 128:(tt + 1) * 128],
                          p1_t[tt].rearrange("t (c f) -> t c f", c=FC),
                          waits=[ap_tok[tt + 1], life["fA"]], transpose=True)
                life["p1"][tt] = t
                xn_tp.append(t)
            xn_done = list(xn_tp)
            life["p1u"] = xn_done
            barrier()
            if debug and mt == 0:
                dbg_hold["fA"] = E.dma("sync", s_sy, dbg["dbg_xnF"], fA[:, :],
                                       waits=[xn_done])

            if mt == 0:
                prep_attn("wv")
                s_last = None
                for oc in range(FC):
                    buf, ldtok, sl = attn_wblock("wv", oc)
                    last = None
                    for ic in range(FC):
                        last = mm(psml[:, oc:oc + 1],
                                  buf[:, ic * 128:(ic + 1) * 128],
                                  c_yt[:, ic * 16 + oc:ic * 16 + oc + 1],
                                  start=(ic == 0), stop=(ic == FC - 1),
                                  waits=[ldtok, tok_yt] if ic == 0 else ())
                    wA_free[sl] = last
                    s_last = last
                s_last = mm(psml2[:, 0:16], buf[:, 0:128], c_yt[:, 0:16],
                            True, True, waits=[s_last])
                s_last = mm(psml2[:, 16:32], buf[:, 0:128], c_yt[:, 0:16],
                            True, True)
                tok_state = pread("vector", s_dv, lambda eng:
                                  eng.tensor_copy(c_state[:, :], psml[:, 0:16]),
                                  [s_last])
                for _k in range(8):
                    tok_state = E.op("vector", s_dv, lambda eng, k=_k:
                                     eng.tensor_copy(
                                         st_a[:, 49 + (k % 2):50 + (k % 2)],
                                         st_a[:, 48:49]))
                life["state"] = tok_state
                if debug:
                    E.dma("sync", s_sy, dbg["dbg_state"], c_state[:, :],
                          waits=[tok_state])
                    E.dma("sync", s_sy, dbg["dbg_lam"], c_lam[:, :], waits=[tok_lam])

            scan_tok = [None] * FC
            for oc in range(FC):
                buf, ldtok, sl = attn_wblock("wv", oc)
                ov, gtok, s = proj_group(buf, fA, extra_waits=[ldtok, xn_done])
                wA_free[sl] = gtok
                ev = pread("vector", s_dv, lambda eng, o=oc, pv=ov:
                           eng.tensor_copy(vF[:, o * T:(o + 1) * T], pv),
                           [gtok, life["vF"], dbg_hold.get("fuse")])
                slot_free[s] = ev
                sc = E.op("vector", s_dv, lambda eng, o=oc:
                          eng.tensor_tensor_scan(
                              retF[:, o * T:(o + 1) * T],
                              c_lam[:, o:o + 1].broadcast_to([128, T]),
                              vF[:, o * T:(o + 1) * T],
                              c_state[:, o:o + 1], AL.mult, AL.add),
                          waits=[tok_lam, life["state"], life["retF"]])
                if oc > 0:
                    scan_tok[oc - 1] = E.op(
                        "vector", s_dv, lambda eng, o=oc - 1:
                        eng.tensor_copy(c_state[:, o:o + 1],
                                        retF[:, (o + 1) * T - 1:(o + 1) * T]))
                if oc == FC - 1:
                    E.op("vector", s_dv, lambda eng, b=0:
                         eng.tensor_copy(st_a[:, 49:50], st_a[:, 48:49]))
                    scan_tok[FC - 1] = E.op(
                        "vector", s_dv, lambda eng, o=FC - 1:
                        eng.tensor_copy(c_state[:, o:o + 1],
                                        retF[:, (o + 1) * T - 1:(o + 1) * T]))
            life["state"] = scan_tok[-1]
            if debug and mt == 0:
                dbg_hold["vF"] = E.dma("sync", s_sy, dbg["dbg_v"], vF,
                                       waits=[scan_tok[-1]])
                dbg_hold["retF"] = E.dma("sync", s_sy, dbg["dbg_ret"], retF,
                                         waits=[scan_tok[-1]])

            barrier()
            if mt == 0:
                prep_attn("wq")
            h_tok = [None] * FC
            for oc in range(FC):
                buf, ldtok, sl = attn_wblock("wq", oc)
                ov, gtok, s = proj_group(buf, fA, extra_waits=[ldtok, xn_done])
                wA_free[sl] = gtok
                ht = pread("vector", s_dv, lambda eng, o=oc, pv=ov:
                           eng.tensor_tensor(hF[:, o * T:(o + 1) * T], pv,
                                             retF[:, o * T:(o + 1) * T], AL.mult),
                           [gtok, scan_tok[oc], dbg_hold.get("vF")])
                slot_free[s] = ht
                h_tok[oc] = ht

            if mt == 0:
                prep_attn("wg")
            g_tok = [None] * FC
            g_last_mm = None
            for oc in range(FC):
                buf, ldtok, sl = attn_wblock("wg", oc)
                ov, gtok, s = proj_group(buf, fA, extra_waits=[ldtok, xn_done])
                wA_free[sl] = gtok
                g_last_mm = gtok
                gt = pread("scalar", s_ac, lambda eng, o=oc, pv=ov:
                           eng.activation(gF[:, o * T:(o + 1) * T], pv, AF.Silu),
                           [gtok, life["gF"]])
                slot_free[s] = gt
                g_tok[oc] = gt

            barrier()
            h_tok.append(vsettle())
            h_tp = []
            for fc in range(FC):
                t = E.dma("scalar", s_tp,
                          p2_rT[:, :, fc * 128:(fc + 1) * 128],
                          hF[:, fc * T:(fc + 1) * T].rearrange("f (n t) -> f n t", n=NTT),
                          waits=[h_tok[fc + 1], life["p2u"], dbg_hold.get("p2x")],
                          transpose=True)
                h_tp.append(t)
            h_tp_done = list(h_tp)
            mu_c, rs_c, _, lna_rows = ln_group([(p2_t[tt], h_tp_done) for tt in range(NTT)])
            lna_ap = []
            for tt in range(NTT):
                E.op("vector", s_dv,
                     lambda eng, j=tt, m=mu_c[:, tt:tt + 1], r=rs_c[:, tt:tt + 1]:
                     eng.tensor_scalar(p1_t[j], p2_t[j], m, r, AL.subtract, AL.mult),
                     waits=[life["p1"][tt], life["p1u"], lna_rows, tok_gamc, tok_betc])
                E.op("vector", s_dv, lambda eng, j=tt:
                     eng.tensor_tensor(t_tmp1[:, :], p1_t[j], c_gam[:, :], AL.mult))
                t = E.op("vector", s_dv, lambda eng, j=tt:
                         eng.tensor_tensor(p1_t[j], t_tmp1[:, :], c_bet[:, :], AL.add))
                lna_ap.append(t)
            life["p2u"] = lna_ap[-1]
            lna_ap.append(vsettle())
            lna_tp = []
            for tt in range(NTT):
                t = E.dma("scalar", s_tp,
                          fA_r[:, :, tt * 128:(tt + 1) * 128],
                          p1_t[tt].rearrange("t (c f) -> t c f", c=FC),
                          waits=[lna_ap[tt + 1], g_last_mm, dbg_hold.get("fA")],
                          transpose=True)
                life["p1"][tt] = t
                lna_tp.append(t)
            lna_done = list(lna_tp)
            life["p1u"] = lna_done
            h2_tok = [None] * FC
            for oc in range(FC):
                t = E.op("vector", s_dv, lambda eng, o=oc:
                         eng.tensor_tensor(h2F[:, o * T:(o + 1) * T],
                                           fA[:, o * T:(o + 1) * T],
                                           gF[:, o * T:(o + 1) * T], AL.mult),
                         waits=[lna_done, g_tok[oc], dbg_hold.get("retF")])
                h2_tok[oc] = t
            h2_tok.append(vsettle())
            life["gF"] = h2_tok[-1]
            if debug and mt == 0:
                dbg_hold["h2F"] = E.dma("sync", s_sy, dbg["dbg_h2"], h2F,
                                        waits=[h2_tok[-1]])

            barrier()
            if mt == 0:
                prep_attn("wo")
            ao_tp = []
            for oc in range(FC):
                buf, ldtok, sl = attn_wblock("wo", oc)
                ov, gtok, s = proj_group(buf, h2F, extra_waits=[ldtok, h2_tok[-1]])
                wA_free[sl] = gtok
                asl = oc % 2
                av = aoc[:, asl * T:(asl + 1) * T]
                et = pread("scalar", s_ac, lambda eng, pv=ov, a=av:
                           eng.copy(a, pv), [gtok, life["aoc"][asl]])
                slot_free[s] = et
                et = E.op("scalar", s_ac, lambda eng:
                          eng.copy(act_dump[:, 1024:2048], act_dump[:, 0:1024]))
                t = E.dma("scalar", s_tp,
                          p1_rT[:, :, oc * 128:(oc + 1) * 128],
                          av.rearrange("f (n t) -> f n t", n=NTT),
                          waits=[et, life["p1u"]], transpose=True)
                life["aoc"][asl] = t
                ao_tp.append(t)
            ao_done = list(ao_tp)
            life["p1u"] = ao_done

            barrier()
            mu_c, rs_c, negms_c, ln2_rows = ln_group(
                [(p1_t[tt], ao_done) for tt in range(NTT)])
            res_tok = []
            for tt in range(NTT):
                at = E.op("vector", s_dv,
                          lambda eng, j=tt, m=mu_c[:, tt:tt + 1], r=rs_c[:, tt:tt + 1]:
                          eng.tensor_scalar(t_tmp2[:, :], p1_t[j], m, r,
                                            AL.subtract, AL.mult),
                          waits=[ln2_rows, life["t2"]])
                rt = E.op("vector", s_dv, lambda eng, j=tt:
                          eng.tensor_tensor(xT_t[j], xT_t[j], t_tmp2[:, :], AL.add),
                          waits=[at])
                life["t2"] = rt
                life["p1"][tt] = at
                res_tok.append(rt)
            if debug and mt == 0:
                cp = E.op("vector", s_dv, lambda eng:
                          eng.tensor_copy(p2[:, :], xT[:, :]), waits=[res_tok[-1]])
                dbg_hold["p2x"] = E.dma("sync", s_sy, dbg["dbg_x2"], p2[:, :],
                                        waits=[cp])

            barrier()
            mu_c, rs_c, _, _ = ln_group(
                [(xT_t[tt], res_tok[tt]) for tt in range(NTT)])
            ap2_tok = []
            for tt in range(NTT):
                t = E.op("vector", s_dv,
                         lambda eng, j=tt, m=mu_c[:, tt:tt + 1], r=rs_c[:, tt:tt + 1]:
                         eng.tensor_scalar(p1_t[j], xT_t[j], m, r, AL.subtract, AL.mult),
                         waits=[life["p1"][tt]])
                ap2_tok.append(t)
            ap2_tok.append(vsettle())
            xn2_tp = []
            for tt in range(NTT):
                t = E.dma("scalar", s_tp,
                          fA_r[:, :, tt * 128:(tt + 1) * 128],
                          p1_t[tt].rearrange("t (c f) -> t c f", c=FC),
                          waits=[ap2_tok[tt + 1], h2_tok[-1]], transpose=True)
                life["p1"][tt] = t
                xn2_tp.append(t)
            xn2_done = list(xn2_tp)
            life["p1u"] = xn2_done

            barrier()
            if mt == 0:
                prep_ffn()
            fuse_tok = [None] * 48
            last_up_mm = None
            for b in range(24):
                sl = b % 2
                gbuf = wG[:, sl * 4096:(sl + 1) * 4096]
                ubuf = wU[:, sl * 4096:(sl + 1) * 4096]
                g_ld = E.dma("sync", s_sy, gbuf, s_g8[b * 128:(b + 1) * 128, :],
                             waits=[life["wG"][sl], f8_store_tok["g"][b]])
                u_ld = E.dma("sync", s_sy, ubuf, s_u8[b * 128:(b + 1) * 128, :],
                             waits=[life["wU"][sl], f8_store_tok["u"][b]])
                for j in range(2):
                    ick = b * 2 + j
                    sg = next_slot()
                    ovg = pm[sg][:, 0:512]
                    lastg = None
                    for ic in range(FC):
                        lastg = mm(ovg,
                                   gbuf[:, ic * 256 + j * 128:ic * 256 + (j + 1) * 128],
                                   fA[:, ic * T:(ic + 1) * T],
                                   start=(ic == 0), stop=(ic == FC - 1),
                                   waits=[g_ld, xn2_done, slot_free[sg]] if ic == 0 else ())
                    gsl = ick % 2
                    gv = gtmp[:, gsl * T:(gsl + 1) * T]
                    gt = pread("scalar", s_ac,
                               lambda eng, pv=ovg, g=gv, sc=c_gs[:, ick:ick + 1]:
                               eng.activation(g, pv, AF.Silu, scale=sc),
                               [lastg, life["gtmp"][gsl], tok_gs])
                    gt = E.op("scalar", s_ac, lambda eng:
                              eng.copy(act_dump[:, 1024:1536], act_dump[:, 0:512]))
                    slot_free[sg] = gt
                    su = next_slot()
                    ovu = pm[su][:, 0:512]
                    lastu = None
                    for ic in range(FC):
                        lastu = mm(ovu,
                                   ubuf[:, ic * 256 + j * 128:ic * 256 + (j + 1) * 128],
                                   fA[:, ic * T:(ic + 1) * T],
                                   start=(ic == 0), stop=(ic == FC - 1),
                                   waits=[u_ld, slot_free[su]] if ic == 0 else ())
                    ft = pread("vector", s_dv,
                               lambda eng, pv=ovu, g=gv, k=ick, sc=c_us[:, ick:ick + 1]:
                               eng.scalar_tensor_tensor(fuse[:, k * T:(k + 1) * T],
                                                        pv, sc, g, AL.mult, AL.mult),
                               [lastu, gt, life["fuse"], tok_us,
                                dbg_hold.get("h2F")])
                    slot_free[su] = ft
                    life["gtmp"][gsl] = ft
                    fuse_tok[ick] = ft
                    last_up_mm = lastu
                life["wG"][sl] = last_up_mm
                life["wU"][sl] = last_up_mm
            fuse_done = vsettle()
            if debug and mt == 0:
                dbg_hold["fuse"] = E.dma("sync", s_sy, dbg["dbg_fuse"], fuse,
                                         waits=[fuse_done])

            barrier()
            do_tp = []
            down_last_mm = None
            for ob in range(NDB):
                sl = ob % 2
                dbuf = wD[:, sl * 6144:(sl + 1) * 6144]
                d_ld = E.dma("sync", s_sy, dbuf, s_d8[ob * 128:(ob + 1) * 128, :],
                             waits=[life["wD"][sl], f8_store_tok["d"][ob]])
                oc = ob
                s = next_slot()
                ov = pm[s][:, 0:512]
                last = None
                for ick in range(48):
                    last = mm(ov, dbuf[:, ick * 128:(ick + 1) * 128],
                              fuse[:, ick * T:(ick + 1) * T],
                              start=(ick == 0), stop=(ick == 47),
                              waits=[d_ld, fuse_done, slot_free[s]] if ick == 0 else ())
                down_last_mm = last
                life["wD"][sl] = last
                asl = oc % 2
                av = aoc[:, asl * T:(asl + 1) * T]
                et = pread("vector", s_dv,
                           lambda eng, pv=ov, a=av, sc=c_ds[:, oc:oc + 1]:
                           eng.tensor_scalar(a, pv, sc, 0.0, AL.mult, AL.add),
                           [last, life["aoc"][asl], tok_ds])
                slot_free[s] = et
                et = E.op("vector", s_dv, lambda eng:
                          eng.tensor_copy(act_dump[:, 1024:2048].bitcast(F32),
                                          act_dump[:, 0:1024].bitcast(F32)))
                t = E.dma("scalar", s_tp,
                          p1_rT[:, :, oc * 128:(oc + 1) * 128],
                          av.rearrange("f (n t) -> f n t", n=NTT),
                          waits=[et, life["p1u"]], transpose=True)
                life["aoc"][asl] = t
                do_tp.append(t)
            do_done = list(do_tp)
            life["p1u"] = do_done
            life["vF"] = down_last_mm
            life["retF"] = down_last_mm
            life["gF"] = down_last_mm
            life["fuse"] = down_last_mm
            life["fA"] = last_up_mm

            barrier()
            mu_c, rs_c, negms_c, ln3_rows = ln_group(
                [(p1_t[tt], do_done) for tt in range(NTT)])
            for tt in range(NTT):
                at = E.op("vector", s_dv,
                          lambda eng, j=tt, m=mu_c[:, tt:tt + 1], r=rs_c[:, tt:tt + 1]:
                          eng.tensor_scalar(t_tmp2[:, :], p1_t[j], m, r,
                                            AL.subtract, AL.mult),
                          waits=[ln3_rows, life["t2"]])
                rt = E.op("vector", s_dv, lambda eng, j=tt:
                          eng.tensor_tensor(xT_t[j], xT_t[j], t_tmp2[:, :], AL.add),
                          waits=[at, dbg_hold.get("xT")])
                life["t2"] = rt
                life["p1"][tt] = at
                st = E.dma("sync", s_sy, out_r[mt * NTT + tt, :, :], xT_t[tt],
                           waits=[rt])
                life["xT"][tt] = st

        for qe in ("sync", "gpsimd", "scalar"):
            for lane in E.lanes[qe]:
                E.on(qe, lambda eng, s=lane, v=lane.n: eng.wait_ge(s.h, v))

        block = es.enter_context(nc.Block())

        @block.sync
        def _(eng):
            for f in E.ops["sync"]:
                f(eng)

        @block.gpsimd
        def _(eng):
            for f in E.ops["gpsimd"]:
                f(eng)

        @block.vector
        def _(eng):
            for f in E.ops["vector"]:
                f(eng)

        @block.scalar
        def _(eng):
            for f in E.ops["scalar"]:
                f(eng)

        @block.tensor
        def _(eng):
            for f in E.ops["tensor"]:
                f(eng)

    return nc


_CACHE = {}


def _get_nc(debug=False):
    key = bool(debug)
    if key not in _CACHE:
        _CACHE[key] = build_program(debug=debug)
    return _CACHE[key]


def make_in_maps(inputs):
    g = {k: np.ascontiguousarray(np.asarray(v, dtype=np.float32))
         for k, v in inputs.items()}
    x = g["L0_attn_x"]
    beta_b = np.ascontiguousarray(
        np.broadcast_to(g["L0_attn_beta"][None, :], (128, NH)))
    gam_b = np.ascontiguousarray(
        np.broadcast_to(g["L0_attn_ln_gamma"][None, :], (128, D)))
    bet_b = np.ascontiguousarray(
        np.broadcast_to(g["L0_attn_ln_beta"][None, :], (128, D)))
    gs_pp = np.ascontiguousarray(g["L0_ffn_gate_scale"].reshape(48, 128).T)
    us_pp = np.ascontiguousarray(g["L0_ffn_up_scale"].reshape(48, 128).T)
    ds_pp = np.ascontiguousarray(g["L0_ffn_down_scale"].reshape(16, 128).T)
    zeros_pre = np.zeros((CT, D), dtype=np.float32)
    common = dict(
        wq=g["L0_attn_Wq"], wv=g["L0_attn_Wv"], wg=g["L0_attn_Wg"], wo=g["L0_attn_Wo"],
        gate_w=g["L0_ffn_gate_w"], up_w=g["L0_ffn_up_w"], down_w=g["L0_ffn_down_w"],
        beta_b=beta_b, gam_b=gam_b, bet_b=bet_b,
        gs_pp=gs_pp, us_pp=us_pp, ds_pp=ds_pp,
    )
    in_maps = []
    for core in range(8):
        b, h = core // 2, core % 2
        m = dict(common)
        m["x_main"] = np.ascontiguousarray(x[b, h * CT:(h + 1) * CT, :])
        m["x_pre"] = zeros_pre if h == 0 else np.ascontiguousarray(x[b, 0:CT, :])
        in_maps.append(m)
    return in_maps


def kernel(**inputs):
    nc = _get_nc(debug=False)
    in_maps = make_in_maps(inputs)
    res = run_bass_kernel_spmd(nc, in_maps, core_ids=list(range(8)))
    out = np.empty((4, 4096, D), dtype=np.float32)
    for core in range(8):
        b, h = core // 2, core % 2
        out[b, h * CT:(h + 1) * CT, :] = np.asarray(res.results[core]["out"])
    return out
